# version 11
# baseline (speedup 1.0000x reference)
import contextlib

import numpy as np

import concourse.bass as bass
import concourse.mybir as mybir
from concourse.bass_utils import run_bass_kernel_spmd

F32 = mybir.dt.float32
BF16 = mybir.dt.bfloat16
AF = mybir.ActivationFunctionType
ALU = mybir.AluOpType

B = 2
SEQ = 8192
H = 1024
I = 2048
FH = 4096
KTAP = 4
NCORE = 8
HC = H // 128
IC = I // 128
FC = FH // 128
EPS = 1e-5


def lhsT_panels(w):
    M, K = w.shape
    a = w.reshape(M // 128, 128, K // 128, 128)
    return np.ascontiguousarray(a.transpose(0, 3, 2, 1))


def rhs_chunks(w):
    N, K = w.shape
    a = w.reshape(N, K // 128, 128)
    return np.ascontiguousarray(a.transpose(1, 2, 0))


def conv_panels(w):
    a = w.reshape(16, 128, 2, 128, KTAP)
    return np.ascontiguousarray(a.transpose(0, 3, 2, 4, 1))


def vec128(v):
    return np.ascontiguousarray(np.asarray(v).reshape(-1, 128).T)


TRUNC = None


def build_program(TOK=2048):
    trunc = TRUNC
    HALO = 128
    TOKH = TOK + HALO
    HN = TOK + KTAP
    T = min(1024, TOK)
    NI = TOK // T
    NT = min(512, T)
    nt_ffn = T // NT
    n_xtiles = TOKH // 128
    n_nt = TOK // NT
    n_tc = TOK // 128
    n_grp = (n_tc + 7) // 8

    nc = bass.Bass("TRN2", target_bir_lowering=False, debug=False)

    x_d = nc.dram_tensor("x", [TOKH, H], F32, kind="ExternalInput").ap()
    inproj_d = nc.dram_tensor("inprojT", [32, 128, HC, 128], F32, kind="ExternalInput").ap()
    convw_d = nc.dram_tensor("convP", [16, 128, 2, KTAP, 128], F32, kind="ExternalInput").ap()
    fc_d = nc.dram_tensor("fcT", [HC, 128, IC, 128], F32, kind="ExternalInput").ap()
    cproj_d = nc.dram_tensor("cprojT", [IC, 128, HC, 128], F32, kind="ExternalInput").ap()
    outw_d = nc.dram_tensor("outwR", [IC, 128, H], F32, kind="ExternalInput").ap()
    w12_d = nc.dram_tensor("w12T", [FC, 128, 2, HC, 128], F32, kind="ExternalInput").ap()
    h1_d = nc.dram_tensor("h1T", [FC, 128, FC, 128], F32, kind="ExternalInput").ap()
    h2_d = nc.dram_tensor("h2T", [FC, 128, FC, 128], F32, kind="ExternalInput").ap()
    w3_d = nc.dram_tensor("w3R", [FC, 128, H], F32, kind="ExternalInput").ap()
    w3sb_d = nc.dram_tensor("w3sb", [128, H], F32, kind="ExternalInput").ap()
    params_d = nc.dram_tensor("params", [128, 200], F32, kind="ExternalInput").ap()
    consts_d = nc.dram_tensor("consts", [128, 8], F32, kind="ExternalInput").ap()
    pm_d = nc.dram_tensor("pm", [1, 256], F32, kind="ExternalInput").ap()
    onesf_d = nc.dram_tensor("onesf", [128, 1], F32, kind="ExternalInput").ap()
    identf_d = nc.dram_tensor("identf", [128, 128], F32, kind="ExternalInput").ap()
    out_d = nc.dram_tensor("out", [TOK, H], F32, kind="ExternalOutput").ap()
    scr_d = nc.dram_tensor("scr", [TOK, H], F32, kind="Internal").ap()

    es = contextlib.ExitStack()

    RA = 0
    SZ_RA = HC * TOKH
    RB = RA + SZ_RA
    SZ_RB = -(-max(FC * T, IC * HN) // 512) * 512
    RB2 = RB + SZ_RB
    SZ_RB2 = -(-max(FC * T, IC * TOK) // 512) * 512
    RINGS = RB2 + SZ_RB2
    r_w12 = RINGS
    r_h12 = RINGS + 4096
    r_w3 = RINGS + 10240
    r_wsb = RINGS + 12288
    r_sil = RINGS + 13312
    r_sq = RINGS + 14336
    SZ_RINGS = 15360
    r_mxp = RINGS
    r_cvp = RINGS + 2048
    r_fcp = RINGS + 4096
    r_cpp = RINGS + 8192
    r_owp = RINGS + 10240
    r_xnt = RB2
    BF_END = RINGS + SZ_RINGS

    ab = es.enter_context(nc.sbuf_tensor("ab", [128, BF_END], BF16))

    AF32 = 3072
    af = es.enter_context(nc.sbuf_tensor("af", [128, AF32], F32))
    f_xt = 0
    f_stage = 0
    f_rows = 0
    f_v = 1536
    f_acc = 0
    f_xrd = 1024
    f_ost = 2048

    params = es.enter_context(nc.sbuf_tensor("params_s", [128, 200], F32))
    consts = es.enter_context(nc.sbuf_tensor("consts_s", [128, 8], F32))
    pm = es.enter_context(nc.sbuf_tensor("pm_s", [1, 256], F32))
    onesb = es.enter_context(nc.sbuf_tensor("onesb", [128, 1], BF16))
    identb = es.enter_context(nc.sbuf_tensor("identb", [128, 128], BF16))
    statmv = es.enter_context(nc.sbuf_tensor("statmv", [128, 2 * n_xtiles], F32))
    statr = es.enter_context(nc.sbuf_tensor("statr", [128, n_xtiles], F32))
    bn6 = es.enter_context(nc.sbuf_tensor("bn6", [128, 12], F32))

    P_CONVB, P_LNG, P_LNB, P_FCB, P_CPB = 0, 16, 32, 48, 56
    P_W1S, P_W2S, P_H1S, P_H2S = 72, 104, 136, 168

    nc.const_aps.register(F32, EPS, consts[:, 0:1])
    nc.const_aps.register(F32, 0.0, consts[:, 1:2])
    nc.const_aps.register(F32, 1.0, consts[:, 2:3])

    def xn(k, c0, n):
        return ab[:, RA + k * TOKH + c0: RA + k * TOKH + c0 + n]

    def u_t(k, c0, n):
        return ab[:, RA + k * TOK + c0: RA + k * TOK + c0 + n]

    def hh_t(k, c0, n):
        return ab[:, RB + k * T + c0: RB + k * T + c0 + n]

    def g_t(k, c0, n):
        return ab[:, RB2 + k * T + c0: RB2 + k * T + c0 + n]

    def h_t(k, c0, n):
        return ab[:, RB + k * HN + c0: RB + k * HN + c0 + n]

    def hg_t(k, c0, n):
        return ab[:, RB + k * TOK + c0: RB + k * TOK + c0 + n]

    def rc_t(k, c0, n):
        return ab[:, RB2 + k * TOK + c0: RB2 + k * TOK + c0 + n]

    def ring(base, slot, n):
        return ab[:, base + slot * n: base + (slot + 1) * n]

    class Ctr:
        def __init__(self, name):
            self.h = es.enter_context(nc.semaphore(name))
            self.n = 0

        def inc(self, instr, k=1):
            instr.then_inc(self.h, k)
            self.n += k
            return self.n

    NWQ = 6
    NSQ = 4
    NAQ = 4
    s_wq = [Ctr(f"s_w{r}") for r in range(NWQ)]
    s_sq = [Ctr(f"s_s{r}") for r in range(NSQ)]
    s_aq = [Ctr(f"s_a{r}") for r in range(NAQ)]
    s_t = Ctr("s_t")
    s_v = Ctr("s_v")
    s_a = Ctr("s_a")

    block = es.enter_context(nc.Block())

    mk = {}
    _waited = {}

    def _wait_ge(eng, sem, val):
        key = (eng.engine, id(sem))
        if _waited.get(key, -1) >= val:
            return
        _waited[key] = val
        eng.wait_ge(sem, val)

    def mark(name, ctr, val=None):
        mk[name] = (ctr, ctr.n if val is None else val)

    def wait(eng, name):
        ctr, val = mk[name]
        _wait_ge(eng, ctr.h, val)

    wplan = []
    w_index = {}

    def wadd(key, dst, src, start_el, n_el):
        w_index[key] = len(wplan)
        wplan.append((key, dst, src, start_el, start_el + n_el))

    def w_wait(eng, key):
        idx = w_index[key]
        _wait_ge(eng, s_wq[idx % NWQ].h, 16 * (idx // NWQ + 1))

    wadd("ones", onesb[:, :], onesf_d, -1, 0)
    wadd("ident", identb[:, :], identf_d, -2, 0)
    wadd("w3sb", ab[:, r_wsb:r_wsb + H], w3sb_d, r_wsb, H)
    h12_ctr = [0]
    for it in range(NI):
        for m in range(FC):
            sl = m % 2
            wadd(f"w12_{it}_{m}", ring(r_w12, sl, 2048), w12_d[m],
                 r_w12 + sl * 2048, 2048)
        for hw, hd in (("h1", h1_d), ("h2", h2_d)):
            for m in range(FC):
                for hf in range(2):
                    sl = h12_ctr[0] % 3
                    h12_ctr[0] += 1
                    w_h12_slot[(hw, it, m, hf)] = sl
                    wadd(f"{hw}_{it}_{m}_{hf}", ring(r_h12, sl, 2048),
                         hd[m, :, hf * 16:(hf + 1) * 16, :],
                         r_h12 + sl * 2048, 2048)
        for half in range(2):
            for kc in range(FC):
                sl = kc % 4
                wadd(f"w3_{it}_{half}_{kc}", ring(r_w3, sl, 512),
                     w3_d[kc, :, half * 512:(half + 1) * 512],
                     r_w3 + sl * 512, 512)
    if trunc not in ("B",):
        for m in range(IC):
            wadd(f"iph_{m}", ring(r_mxp, m % 2, 1024), inproj_d[m],
                 r_mxp + (m % 2) * 1024, 1024)
        for oc in range(16):
            wadd(f"cv_{oc}", ring(r_cvp, oc % 2, 1024), convw_d[oc],
                 r_cvp + (oc % 2) * 1024, 1024)
        for m in range(IC):
            wadd(f"ipg_{m}", ring(r_mxp, m % 2, 1024), inproj_d[IC + m],
                 r_mxp + (m % 2) * 1024, 1024)
    if trunc not in ("B", "E"):
        for nt in range(n_nt):
            for m in range(HC):
                wadd(f"fc_{nt}_{m}", ring(r_fcp, m % 2, 2048), fc_d[m],
                     r_fcp + (m % 2) * 2048, 2048)
            for m in range(IC):
                wadd(f"cp_{nt}_{m}", ring(r_cpp, m % 2, 1024), cproj_d[m],
                     r_cpp + (m % 2) * 1024, 1024)
    n_g4 = (n_tc + 3) // 4
    if trunc is None:
        _fin = []
        for grp in range(n_g4):
            for half in range(2):
                for tc in range(grp * 4, min((grp + 1) * 4, n_tc)):
                    _fin.append((grp, half, tc))
        for grp in range(n_g4):
            for half in range(2):
                for kc in range(IC):
                    sl = kc % 4
                    wadd(f"ow_{grp}_{half}_{kc}", ring(r_owp, sl, 512),
                         outw_d[kc, :, half * 512:(half + 1) * 512],
                         r_owp + sl * 512, 512)
                for tc in range(grp * 4, min((grp + 1) * 4, n_tc)):
                    ii = _fin.index((grp, half, tc))
                    wadd(f"acc_{tc}_{half}",
                         af[:, f_acc + (ii % 2) * 512: f_acc + (ii % 2) * 512 + 512],
                         scr_d[tc * 128:(tc + 1) * 128,
                               half * 512:(half + 1) * 512], -1, 0)
                    wadd(f"xr_{tc}_{half}",
                         af[:, f_xrd + (ii % 2) * 512: f_xrd + (ii % 2) * 512 + 512],
                         x_d[HALO + tc * 128: HALO + (tc + 1) * 128,
                             half * 512:(half + 1) * 512], -1, 0)

    splan = []
    s_qidx = {}

    def sadd(key, dst, src, kind="load", data_mark=None):
        s_qidx[key] = len(splan)
        splan.append((key, kind, dst, src, data_mark))

    def sload_wait(eng, key):
        idx = s_qidx[key]
        _wait_ge(eng, s_sq[idx % NSQ].h, 16 * (idx // NSQ + 1))

    sadd("params", params[:, :], params_d)
    sadd("consts", consts[:, :], consts_d)
    sadd("pm", pm[:, :], pm_d)
    for i in range(n_xtiles):
        sadd(f"xt{i}",
             af[:, f_xt + (i % 2) * 1024: f_xt + (i % 2) * 1024 + H],
             x_d[i * 128:(i + 1) * 128, :])

    def store_emit(scalar, key, dst, src, data_mark):
        if key not in s_qidx:
            sadd(key, dst, src, kind="store", data_mark=data_mark)

    def saq_wait(eng, key):
        idx = s_qidx[key]
        _wait_ge(eng, s_sq[idx % NSQ].h, 16 * (idx // NSQ + 1))

    def saq_wait_all(eng, key):
        idx = s_qidx[key]
        for r in range(NSQ):
            cnt = (idx - r) // NSQ + 1 if idx >= r else 0
            if cnt > 0:
                _wait_ge(eng, s_sq[r].h, 16 * cnt)

    def emit_tile(i):
        @block.vector
        def _(vector, i=i):
            sload_wait(vector, f"xt{i}")
            xt = af[:, f_xt + (i % 2) * 1024: f_xt + (i % 2) * 1024 + H]
            vector.drain()
            vector.bn_stats(bn6[:, 0:6], xt[:, 0:512])
            vector.bn_stats(bn6[:, 6:12], xt[:, 512:1024])
            vector.drain()
            ins = vector.bn_aggr(statmv[:, 2 * i:2 * i + 2], bn6[:, :])
            mark(f"stats{i}", s_v, s_v.inc(ins))

        @block.scalar
        def _(scalar, i=i):
            if i == 0:
                sload_wait(scalar, "consts")
            wait(scalar, f"stats{i}")
            scalar.drain()
            scalar.activation(statr[:, i:i + 1], statmv[:, 2 * i + 1:2 * i + 2],
                              AF.Ln, bias=EPS, scale=1.0)
            scalar.drain()
            ins = scalar.activation(statr[:, i:i + 1], statr[:, i:i + 1],
                                    AF.Exp, bias=0.0, scale=-0.5)
            mark(f"rsq{i}", s_a, s_a.inc(ins))

        @block.vector
        def _(vector, i=i):
            wait(vector, f"rsq{i}")
            if i >= 2:
                wait(vector, f"tr{i - 2}")
            xt = af[:, f_xt + (i % 2) * 1024: f_xt + (i % 2) * 1024 + H]
            vector.drain()
            ins = vector.tensor_scalar(ring(r_xnt, i % 2, 1024), xt,
                                       statmv[:, 2 * i:2 * i + 1],
                                       statr[:, i:i + 1],
                                       ALU.subtract, ALU.mult)
            mark(f"apply{i}", s_v, s_v.inc(ins))
            mark(f"xt{i}", s_v)

        @block.tensor
        def _(tensor, i=i):
            if i == 0:
                w_wait(tensor, "ident")
            wait(tensor, f"apply{i}")
            if i >= 1:
                wait(tensor, f"evac{i - 1}")
            xnt = ring(r_xnt, i % 2, 1024)
            for k in range(HC):
                ins = tensor.transpose(ps_tr[:, k * 128:(k + 1) * 128],
                                       xnt[:, k * 128:(k + 1) * 128],
                                       identb[:, :])
            mark(f"tr{i}", s_t, s_t.inc(ins))

        @block.vector
        def _(vector, i=i):
            wait(vector, f"tr{i}")
            for k in range(HC):
                ins = vector.tensor_copy(xn(k, i * 128, 128),
                                         ps_tr[:, k * 128:(k + 1) * 128])
            mark(f"evac{i}", s_v, s_v.inc(ins))

    ps_mm = [es.enter_context(nc.psum_tensor(f"ps_mm{j}", [128, 512], F32))
             for j in range(4)]
    ps_tok = [es.enter_context(nc.psum_tensor("ps_tok0", [128, 1024], F32))]
    es_tr = contextlib.ExitStack()
    ps_tr = es_tr.enter_context(nc.psum_tensor("ps_tr", [128, 1024], BF16))

    upfront = min((HALO + T - 1) // 128, n_xtiles - 1)
    for i in range(upfront + 1):
        emit_tile(i)
    pend_tiles = list(range(upfront + 1, n_xtiles))

    XN_READY = f"evac{n_xtiles - 1}"

    bank_mk = {}

    def bank_wait(tensor, bank):
        if bank in bank_mk:
            wait(tensor, bank_mk[bank])

    def bank_done(bank, name):
        bank_mk[bank] = name

    ps8 = None

    spill_order = []

    for it in range(NI):
        c0 = HALO + it * T

        for m in range(FC):
            if it == 0 and m % 3 == 0 and pend_tiles:
                emit_tile(pend_tiles.pop(0))

            @block.tensor
            def _(tensor, m=m, it=it, c0=c0):
                w_wait(tensor, f"w12_{it}_{m}")
                for half in range(2):
                    pan = ring(r_w12, m % 2, 2048)[:, half * 1024:(half + 1) * 1024]
                    pan3 = pan.rearrange("p (kc m) -> p kc m", kc=HC)
                    for t2 in range(nt_ffn):
                        if m == 0:
                            need = (c0 + t2 * NT + NT - 1) // 128
                            wait(tensor, f"evac{need}")
                        bank = 2 * half + t2
                        bank_wait(tensor, bank)
                        for k in range(HC):
                            ins = tensor.matmul(
                                ps_mm[bank][:, 0:NT], pan3[:, k, :],
                                xn(k, c0 + t2 * NT, NT),
                                start=(k == 0), stop=(k == HC - 1))
                    mark(f"w{half + 1}_{it}_{m}", s_t, s_t.inc(ins))
                mark(f"w12_{it}_{m}", s_t, s_t.n)

            @block.scalar
            def _(scalar, m=m, it=it):
                if it == 0 and m == 0:
                    sload_wait(scalar, "params")
                wait(scalar, f"w1_{it}_{m}")
                for t2 in range(nt_ffn):
                    if f"silfree_{t2}" in mk:
                        wait(scalar, f"silfree_{t2}")
                    ins = scalar.activation(
                        ring(r_sil, t2, 512)[:, 0:NT], ps_mm[t2][:, 0:NT],
                        AF.Silu, bias=0.0,
                        scale=params[:, P_W1S + m:P_W1S + m + 1])
                    mark(f"sil_{t2}", s_a, s_a.inc(ins))

            @block.vector
            def _(vector, m=m, it=it):
                if it == 0 and m == 0:
                    sload_wait(vector, "params")
                wait(vector, f"w2_{it}_{m}")
                for t2 in range(nt_ffn):
                    wait(vector, f"sil_{t2}")
                    vector.drain()
                    ins = vector.scalar_tensor_tensor(
                        hh_t(m, t2 * NT, NT), ps_mm[2 + t2][:, 0:NT],
                        params[:, P_W2S + m:P_W2S + m + 1],
                        ring(r_sil, t2, 512)[:, 0:NT], ALU.mult, ALU.mult)
                    v = s_v.inc(ins)
                    mark(f"silfree_{t2}", s_v, v)
                    bank_done(t2, f"silfree_{t2}")
                    mark(f"w2bk_{it}_{m}_{t2}", s_v, v)
                    bank_done(2 + t2, f"w2bk_{it}_{m}_{t2}")
                mark(f"hh_{it}_{m}", s_v, s_v.n)

        if it == 0:
            while pend_tiles:
                emit_tile(pend_tiles.pop(0))
            es_tr.close()
            ps_tok.append(es.enter_context(
                nc.psum_tensor("ps_tok1", [128, 1024], F32)))
            ps8 = [t[:, 0:512] for t in ps_mm] + [
                ps_tok[0][:, 0:512], ps_tok[0][:, 512:1024],
                ps_tok[1][:, 0:512], ps_tok[1][:, 512:1024]]
            bank_mk[6] = XN_READY
            bank_mk[7] = XN_READY

        for hw, psc, src_t, dst_t in (("h1", P_H1S, hh_t, g_t),
                                      ("h2", P_H2S, g_t, hh_t)):
            for m in range(FC):
                @block.tensor
                def _(tensor, m=m, it=it, hw=hw, src_t=src_t):
                    if m == 0:
                        wait(tensor, (f"hh_{it}_{FC - 1}" if hw == "h1"
                                      else f"gg_{it}_{FC - 1}"))
                    for t2 in range(nt_ffn):
                        bank = 2 * (m % 2) + t2
                        bank_wait(tensor, bank)
                        for hf in range(2):
                            w_wait(tensor, f"{hw}_{it}_{m}_{hf}")
                            pan = ring(r_h12, w_h12_slot[(hw, it, m, hf)], 2048)
                            pan3 = pan.rearrange("p (kc m) -> p kc m", kc=16)
                            for kk in range(16):
                                k = hf * 16 + kk
                                ins = tensor.matmul(
                                    ps_mm[bank][:, 0:NT], pan3[:, kk, :],
                                    src_t(k, t2 * NT, NT),
                                    start=(k == 0), stop=(k == FC - 1))
                            v = s_t.inc(ins)
                            if t2 == nt_ffn - 1:
                                mark(f"{hw}_{it}_{m}_{hf}", s_t, v)
                        mark(f"{hw}mm_{it}_{m}_{t2}", s_t, v)

                @block.scalar
                def _(scalar, m=m, it=it, hw=hw, psc=psc, dst_t=dst_t):
                    for t2 in range(nt_ffn):
                        bank = 2 * (m % 2) + t2
                        wait(scalar, f"{hw}mm_{it}_{m}_{t2}")
                        ins = scalar.activation(
                            dst_t(m, t2 * NT, NT), ps_mm[bank][:, 0:NT],
                            AF.Gelu, bias=0.0,
                            scale=params[:, psc + m:psc + m + 1])
                        a = s_a.inc(ins)
                        mark(f"{hw}e_{it}_{m}_{t2}", s_a, a)
                        bank_done(bank, f"{hw}e_{it}_{m}_{t2}")
                    key = "gg" if hw == "h1" else "g2"
                    mark(f"{key}_{it}_{m}", s_a, s_a.n)

        n_tch = T // 128
        for half in range(2):
            for kc in range(FC):
                @block.vector
                def _(vector, it=it, half=half, kc=kc):
                    w_wait(vector, f"w3_{it}_{half}_{kc}")
                    li = (it * 2 + half) * FC + kc
                    if li >= 4:
                        pit, rem = divmod(li - 4, 2 * FC)
                        phf, pkc = divmod(rem, FC)
                        wait(vector, f"w3c_{pit}_{phf}_{pkc}")
                    if it == 0 and half == 0 and kc == 0:
                        w_wait(vector, "w3sb")
                    pan = ring(r_w3, kc % 4, 512)
                    vector.drain()
                    ins = vector.tensor_tensor(
                        pan, pan,
                        ab[:, r_wsb + half * 512: r_wsb + (half + 1) * 512],
                        ALU.mult)
                    mark(f"w3s_{it}_{half}_{kc}", s_v, s_v.inc(ins))

                @block.tensor
                def _(tensor, it=it, half=half, kc=kc, n_tch=n_tch):
                    if half == 0 and kc == 0:
                        wait(tensor, f"g2_{it}_{FC - 1}")
                    wait(tensor, f"w3s_{it}_{half}_{kc}")
                    pan = ring(r_w3, kc % 4, 512)
                    for tc in range(n_tch):
                        if kc == 0:
                            bank_wait(tensor, tc)
                        ins = tensor.matmul(
                            ps8[tc], hh_t(kc, tc * 128, 128), pan,
                            start=(kc == 0), stop=(kc == FC - 1))
                    v = s_t.inc(ins)
                    mark(f"w3c_{it}_{half}_{kc}", s_t, v)
                    mark(f"w3_{it}_{half}_{kc}", s_t, v)
                    if kc == FC - 1:
                        mark(f"w3done_{it}_{half}", s_t, v)

            for tc in range(n_tch):
                spill_order.append(f"spill_{it}_{half}_{tc}")

            for tc in range(n_tch):
                @block.vector
                def _(vector, it=it, half=half, tc=tc, n_tch=n_tch):
                    if tc == 0:
                        wait(vector, f"w3done_{it}_{half}")
                    li = (it * 2 + half) * n_tch + tc
                    slot = li % 4
                    if li >= 4:
                        saq_wait(vector, spill_order[li - 4])
                    st = af[:, f_stage + slot * 512: f_stage + slot * 512 + 512]
                    vector.drain()
                    ins = vector.tensor_copy(st, ps8[tc])
                    v = s_v.inc(ins)
                    mark(f"spill_{it}_{half}_{tc}", s_v, v)
                    mark(f"w3bk_{it}_{half}_{tc}", s_v, v)
                    bank_done(tc, f"w3bk_{it}_{half}_{tc}")

                @block.scalar
                def _(scalar, it=it, half=half, tc=tc, n_tch=n_tch):
                    li = (it * 2 + half) * n_tch + tc
                    slot = li % 4
                    st = af[:, f_stage + slot * 512: f_stage + slot * 512 + 512]
                    store_emit(scalar, f"spill_{it}_{half}_{tc}",
                               scr_d[it * T + tc * 128: it * T + (tc + 1) * 128,
                                     half * 512:(half + 1) * 512],
                               st, f"spill_{it}_{half}_{tc}")

    LAST_SPILL = spill_order[-1]

    def _emit_dma_queues():
        @block.gpsimd
        def _(gpsimd):
            lastblk = {}
            for idx, (key, dst, src, st_el, en_el) in enumerate(wplan):
                r = idx % NWQ
                if idx >= NWQ:
                    gpsimd.wait_ge(s_wq[r].h, 16 * (idx // NWQ))
                if key.startswith("acc_") or key.startswith("xr_"):
                    pre, tc2, half2 = key.split("_")
                    tc2, half2 = int(tc2), int(half2)
                    ii = fin_idx[(tc2 // 4, half2, tc2)]
                    if ii == 0 and pre == "acc":
                        wait(gpsimd, f"bc_{n_nt - 1}")
                        saq_wait_all(gpsimd, LAST_SPILL)
                    if ii == 0 and pre == "xr":
                        wait(gpsimd, f"hna_{n_nt - 1}_{IC - 1}")
                    if ii >= 2:
                        pg, ph, pt = fin_order[ii - 2]
                        wait(gpsimd, f"fin_{pt}_{ph}")
                if st_el >= 0:
                    waits = {lastblk[b] for b in range(st_el // 512,
                                                       (en_el + 511) // 512)
                             if b in lastblk}
                    for wkey in waits:
                        wait(gpsimd, wkey)
                ins = gpsimd.dma_start(out=dst, in_=src)
                s_wq[r].inc(ins, 16)
                if st_el >= 0:
                    for bk in range(st_el // 512, (en_el + 511) // 512):
                        lastblk[bk] = key

        @block.sync
        def _(sync):
            for idx, (key, kind, dst, src, data_mark) in enumerate(splan):
                r = idx % NSQ
                if idx >= NSQ:
                    sync.wait_ge(s_sq[r].h, 16 * (idx // NSQ))
                if kind == "store":
                    wait(sync, data_mark)
                elif key.startswith("xt"):
                    i = int(key[2:])
                    if i >= 2:
                        wait(sync, f"xt{i - 2}")
                ins = sync.dma_start(out=dst, in_=src)
                s_sq[r].inc(ins, 16)
            for r in range(NSQ):
                sync.wait_ge(s_sq[r].h, s_sq[r].n)

    def _finish_truncated():
        _emit_dma_queues()
        es.close()
        return nc

    if trunc == "B":
        return _finish_truncated()

    h_tiles = []
    off = 0
    while off < HN:
        nn = min(512, HN - off)
        h_tiles.append((off, nn))
        off += nn

    for m in range(IC):
        @block.tensor
        def _(tensor, m=m):
            w_wait(tensor, f"iph_{m}")
            pan = ring(r_mxp, m % 2, 1024)
            pan3 = pan.rearrange("p (kc m) -> p kc m", kc=HC)
            for ti, (off, nn) in enumerate(h_tiles):
                bank_wait(tensor, ti)
                for k in range(HC):
                    ins = tensor.matmul(ps8[ti][:, 0:nn], pan3[:, k, :],
                                        xn(k, 124 + off, nn),
                                        start=(k == 0), stop=(k == HC - 1))
                mark(f"iphm_{m}_{ti}", s_t, s_t.inc(ins))
            mark(f"iph_{m}", s_t, s_t.n)

        if m % 2 == 0:
            @block.vector
            def _(vector, m=m):
                for ti, (off, nn) in enumerate(h_tiles):
                    wait(vector, f"iphm_{m}_{ti}")
                    vector.drain()
                    ins = vector.tensor_copy(h_t(m, off, nn),
                                             ps8[ti][:, 0:nn])
                    mark(f"he_{m}_{ti}", s_v, s_v.inc(ins))
                    bank_done(ti, f"he_{m}_{ti}")
                mark(f"h_{m}", s_v, s_v.n)
        else:
            @block.scalar
            def _(scalar, m=m):
                for ti, (off, nn) in enumerate(h_tiles):
                    wait(scalar, f"iphm_{m}_{ti}")
                    ins = scalar.activation(h_t(m, off, nn),
                                            ps8[ti][:, 0:nn],
                                            AF.Copy, bias=0.0, scale=1.0)
                    mark(f"he_{m}_{ti}", s_a, s_a.inc(ins))
                    bank_done(ti, f"he_{m}_{ti}")
                mark(f"h_{m}", s_a, s_a.n)

    for oc in range(16):
        @block.tensor
        def _(tensor, oc=oc):
            w_wait(tensor, f"cv_{oc}")
            g2c = 2 * (oc // 2)
            wait(tensor, f"h_{g2c}")
            wait(tensor, f"h_{g2c + 1}")
            pan = ring(r_cvp, oc % 2, 1024)
            pan4 = pan.rearrange("p (ic k m) -> p ic k m", ic=2, k=KTAP)
            for nt in range(n_nt):
                bank = (oc * n_nt + nt) % 4
                bank_wait(tensor, bank)
                first = True
                for icl in range(2):
                    for k in range(KTAP):
                        ins = tensor.matmul(
                            ps_mm[bank][:, 0:NT], pan4[:, icl, k, :],
                            h_t(g2c + icl, nt * NT + 1 + k, NT),
                            start=first, stop=(icl == 1 and k == KTAP - 1))
                        first = False
                mark(f"cvm_{oc}_{nt}", s_t, s_t.inc(ins))
            mark(f"cv_{oc}", s_t, s_t.n)

        @block.vector
        def _(vector, oc=oc):
            for nt in range(n_nt):
                bank = (oc * n_nt + nt) % 4
                wait(vector, f"cvm_{oc}_{nt}")
                vector.drain()
                ins = vector.tensor_scalar(
                    rc_t(oc, nt * NT, NT), ps_mm[bank][:, 0:NT],
                    params[:, P_CONVB + oc:P_CONVB + oc + 1], 0.0,
                    ALU.add, ALU.add)
                mark(f"cve_{oc}_{nt}", s_v, s_v.inc(ins))
                bank_done(bank, f"cve_{oc}_{nt}")
            mark(f"cvo_{oc}", s_v, s_v.n)

    for m in range(IC):
        for nt in range(n_nt):
            @block.tensor
            def _(tensor, m=m, nt=nt):
                if nt == 0:
                    w_wait(tensor, f"ipg_{m}")
                bank = (m * n_nt + nt) % 4
                bank_wait(tensor, bank)
                pan = ring(r_mxp, m % 2, 1024)
                pan3 = pan.rearrange("p (kc m) -> p kc m", kc=HC)
                for k in range(HC):
                    ins = tensor.matmul(ps_mm[bank][:, 0:NT], pan3[:, k, :],
                                        xn(k, HALO + nt * NT, NT),
                                        start=(k == 0), stop=(k == HC - 1))
                mark(f"ipgm_{m}_{nt}", s_t, s_t.inc(ins))
                if nt == n_nt - 1:
                    mark(f"ipg_{m}", s_t, s_t.n)

            @block.scalar
            def _(scalar, m=m, nt=nt):
                bank = (m * n_nt + nt) % 4
                wait(scalar, f"ipgm_{m}_{nt}")
                sl = (m * n_nt + nt) % 2
                if f"gsilfree_{sl}" in mk:
                    wait(scalar, f"gsilfree_{sl}")
                ins = scalar.activation(ring(r_sil, sl, 512)[:, 0:NT],
                                        ps_mm[bank][:, 0:NT], AF.Silu,
                                        bias=0.0, scale=1.0)
                mark(f"gsil_{m}_{nt}", s_a, s_a.inc(ins))

            @block.vector
            def _(vector, m=m, nt=nt):
                bank = (m * n_nt + nt) % 4
                sl = (m * n_nt + nt) % 2
                wait(vector, f"gsil_{m}_{nt}")
                wait(vector, f"cvo_{m}")
                vector.drain()
                ins = vector.tensor_tensor(hg_t(m, nt * NT, NT),
                                           ring(r_sil, sl, 512)[:, 0:NT],
                                           rc_t(m, nt * NT, NT), ALU.mult)
                v = s_v.inc(ins)
                mark(f"gsilfree_{sl}", s_v, v)
                mark(f"hge_{m}_{nt}", s_v, v)
                bank_done(bank, f"hge_{m}_{nt}")
                if nt == n_nt - 1:
                    mark(f"hg_{m}", s_v, s_v.n)

    if trunc == "E":
        return _finish_truncated()

    inv_n = 1.0 / float(I)

    def emit_fccp(nt):
        for m in range(HC):
            @block.tensor
            def _(tensor, nt=nt, m=m):
                w_wait(tensor, f"fc_{nt}_{m}")
                if m == 0:
                    wait(tensor, f"hna_{nt}_{IC - 1}")
                pan = ring(r_fcp, m % 2, 2048)
                pan3 = pan.rearrange("p (kc m) -> p kc m", kc=IC)
                bank = m % 4
                bank_wait(tensor, bank)
                for k in range(IC):
                    ins = tensor.matmul(ps_mm[bank][:, 0:NT], pan3[:, k, :],
                                        rc_t(k, nt * NT, NT),
                                        start=(k == 0), stop=(k == IC - 1))
                mark(f"fcm_{nt}_{m}", s_t, s_t.inc(ins))
                mark(f"fc_{nt}_{m}", s_t, s_t.n)

            @block.scalar
            def _(scalar, nt=nt, m=m):
                bank = m % 4
                wait(scalar, f"fcm_{nt}_{m}")
                ins = scalar.activation(u_t(m, nt * NT, NT),
                                        ps_mm[bank][:, 0:NT], AF.Silu,
                                        bias=params[:, P_FCB + m:P_FCB + m + 1],
                                        scale=1.0)
                mark(f"ue_{nt}_{m}", s_a, s_a.inc(ins))
                bank_done(bank, f"ue_{nt}_{m}")

        for m in range(IC):
            @block.tensor
            def _(tensor, nt=nt, m=m):
                w_wait(tensor, f"cp_{nt}_{m}")
                if m == 0:
                    wait(tensor, f"ue_{nt}_{HC - 1}")
                pan = ring(r_cpp, m % 2, 1024)
                pan3 = pan.rearrange("p (kc m) -> p kc m", kc=HC)
                bank = m % 4
                bank_wait(tensor, bank)
                for k in range(HC):
                    ins = tensor.matmul(ps_mm[bank][:, 0:NT], pan3[:, k, :],
                                        u_t(k, nt * NT, NT),
                                        start=(k == 0), stop=(k == HC - 1))
                mark(f"cpm_{nt}_{m}", s_t, s_t.inc(ins))
                mark(f"cp_{nt}_{m}", s_t, s_t.n)

            @block.vector
            def _(vector, nt=nt, m=m):
                bank = m % 4
                wait(vector, f"cpm_{nt}_{m}")
                vector.drain()
                ins = vector.scalar_tensor_tensor(
                    rc_t(m, nt * NT, NT), ps_mm[bank][:, 0:NT],
                    params[:, P_CPB + m:P_CPB + m + 1], hg_t(m, nt * NT, NT),
                    ALU.add, ALU.add)
                mark(f"at2e_{nt}_{m}", s_v, s_v.inc(ins))
                bank_done(bank, f"at2e_{nt}_{m}")
        mark(f"at2nt_{nt}", s_v, s_v.n)

    def emit_apply(nt):
        p = nt % 2
        for kc in range(IC):
            @block.vector
            def _(vector, nt=nt, kc=kc, p=p):
                if kc == 0:
                    wait(vector, f"bc_{nt}")
                vs = af[:, f_v + (kc % 2) * 512: f_v + (kc % 2) * 512 + NT]
                li = nt * IC + kc
                if li >= 2:
                    pnt, pkc = divmod(li - 2, IC)
                    wait(vector, f"hna_{pnt}_{pkc}")
                vector.drain()
                vector.tensor_tensor(vs, hg_t(kc, nt * NT, NT),
                                     ps_tok[p][:, 0:NT], ALU.mult)
                vector.drain()
                ins = vector.tensor_tensor(vs, vs, ps_tok[p][:, 512:512 + NT],
                                           ALU.add)
                v = s_v.inc(ins)
                mark(f"vs_{nt}_{kc}", s_v, v)
                if kc == IC - 1:
                    mark(f"tokbk_{nt}", s_v, v)
                    bank_done(4 + 2 * p, f"tokbk_{nt}")
                    bank_done(5 + 2 * p, f"tokbk_{nt}")

            @block.scalar
            def _(scalar, nt=nt, kc=kc):
                wait(scalar, f"vs_{nt}_{kc}")
                vs = af[:, f_v + (kc % 2) * 512: f_v + (kc % 2) * 512 + NT]
                ins = scalar.activation(rc_t(kc, nt * NT, NT), vs, AF.Identity,
                                        bias=params[:, P_LNB + kc:P_LNB + kc + 1],
                                        scale=params[:, P_LNG + kc:P_LNG + kc + 1])
                mark(f"hna_{nt}_{kc}", s_a, s_a.inc(ins))

    for nt in range(n_nt):
        sb, qb = 2 * (nt % 2), 2 * (nt % 2) + 1
        p = nt % 2

        for kc in range(IC):
            @block.vector
            def _(vector, nt=nt, kc=kc):
                if nt == 0:
                    wait(vector, f"hg_{kc}")
                li = nt * IC + kc
                if li >= 2:
                    pnt, pkc = divmod(li - 2, IC)
                    wait(vector, f"sqc_{pnt}_{pkc}")
                vector.drain()
                ins = vector.tensor_tensor(ring(r_sq, kc % 2, 512)[:, 0:NT],
                                           hg_t(kc, nt * NT, NT),
                                           hg_t(kc, nt * NT, NT), ALU.mult)
                mark(f"sq_{nt}_{kc}", s_v, s_v.inc(ins))

            @block.tensor
            def _(tensor, nt=nt, kc=kc, sb=sb, qb=qb):
                if kc == 0:
                    if nt == 0:
                        w_wait(tensor, "ones")
                    bank_wait(tensor, sb)
                    bank_wait(tensor, qb)
                wait(tensor, f"hg_{kc}")
                tensor.matmul(ps_mm[sb][0:1, 0:NT], onesb[:, :],
                              hg_t(kc, nt * NT, NT),
                              start=(kc == 0), stop=(kc == IC - 1))
                wait(tensor, f"sq_{nt}_{kc}")
                ins = tensor.matmul(ps_mm[qb][0:1, 0:NT], onesb[:, :],
                                    ring(r_sq, kc % 2, 512)[:, 0:NT],
                                    start=(kc == 0), stop=(kc == IC - 1))
                mark(f"sqc_{nt}_{kc}", s_t, s_t.inc(ins))
                if kc == IC - 1:
                    mark(f"st2_{nt}", s_t, s_t.n)

        mean = af[0:1, f_rows + 0: f_rows + NT]
        rr = af[0:1, f_rows + 512: f_rows + 512 + NT]
        mr = af[0:1, f_rows + 1024: f_rows + 1024 + NT]

        @block.vector
        def _(vector, nt=nt, sb=sb, qb=qb, mean=mean, rr=rr, mr=mr):
            wait(vector, f"st2_{nt}")
            if nt == 0:
                saq_wait_all(vector, LAST_SPILL)
            if nt >= 1:
                wait(vector, f"bc_{nt - 1}")
            vector.drain()
            vector.tensor_scalar(mean, ps_mm[sb][0:1, 0:NT], inv_n, 0.0,
                                 ALU.mult, ALU.add)
            vector.drain()
            vector.tensor_tensor(mr, mean, mean, ALU.mult)
            vector.drain()
            ins = vector.scalar_tensor_tensor(rr, ps_mm[qb][0:1, 0:NT], inv_n,
                                              mr, ALU.mult, ALU.subtract)
            mark(f"var_{nt}", s_v, s_v.inc(ins))
            mark(f"stat_sb_{nt}", s_v, s_v.n)
            bank_done(sb, f"stat_sb_{nt}")

        @block.scalar
        def _(scalar, nt=nt, rr=rr):
            wait(scalar, f"var_{nt}")
            scalar.drain()
            scalar.activation(rr, rr, AF.Ln, bias=EPS, scale=1.0)
            scalar.drain()
            ins = scalar.activation(rr, rr, AF.Exp, bias=0.0, scale=-0.5)
            mark(f"r_{nt}", s_a, s_a.inc(ins))

        @block.vector
        def _(vector, nt=nt, qb=qb, mean=mean, rr=rr, mr=mr):
            wait(vector, f"r_{nt}")
            vector.drain()
            ins = vector.tensor_tensor(mr, mean, rr, ALU.mult)
            mark(f"rows_{nt}", s_v, s_v.inc(ins))
            bank_done(qb, f"rows_{nt}")

        @block.tensor
        def _(tensor, nt=nt, p=p, rr=rr, mr=mr):
            if nt == 0:
                sload_wait(tensor, "pm")
            wait(tensor, f"rows_{nt}")
            bank_wait(tensor, 4 + 2 * p)
            bank_wait(tensor, 5 + 2 * p)
            tensor.matmul(ps_tok[p][:, 0:NT], pm[:, 0:128], rr,
                          start=True, stop=True)
            ins = tensor.matmul(ps_tok[p][:, 512:512 + NT], pm[:, 128:256], mr,
                                start=True, stop=True)
            mark(f"bc_{nt}", s_t, s_t.inc(ins))

        if nt >= 1:
            emit_apply(nt - 1)
        if nt >= 2:
            emit_fccp(nt - 2)

    emit_apply(n_nt - 1)
    if n_nt >= 2:
        emit_fccp(n_nt - 2)
    emit_fccp(n_nt - 1)

    if trunc == "H":
        return _finish_truncated()

    fin_order = []
    for grp in range(n_g4):
        for half in range(2):
            for tc in range(grp * 4, min((grp + 1) * 4, n_tc)):
                fin_order.append((grp, half, tc))
    fin_idx = {t: i for i, t in enumerate(fin_order)}

    for ii, (grp, half, tc) in enumerate(fin_order):
        sadd(f"ost_{tc}_{half}",
             out_d[tc * 128:(tc + 1) * 128, half * 512:(half + 1) * 512],
             af[:, f_ost + (ii % 2) * 512: f_ost + (ii % 2) * 512 + 512],
             kind="store", data_mark=f"ost_{tc}_{half}")

    for grp in range(n_g4):
        tcs = list(range(grp * 4, min((grp + 1) * 4, n_tc)))
        need_nt = (tcs[-1] * 128 + 127) // NT
        bb = (grp % 2) * 4
        for half in range(2):
            @block.tensor
            def _(tensor, grp=grp, half=half, tcs=tcs, need_nt=need_nt, bb=bb):
                if half == 0:
                    wait(tensor, f"at2nt_{need_nt}")
                for kc in range(IC):
                    w_wait(tensor, f"ow_{grp}_{half}_{kc}")
                    pan = ring(r_owp, kc % 4, 512)
                    for j, tc in enumerate(tcs):
                        if kc == 0:
                            bank_wait(tensor, bb + j)
                        ins = tensor.matmul(
                            ps8[bb + j], rc_t(kc, tc * 128, 128), pan,
                            start=(kc == 0), stop=(kc == IC - 1))
                    v = s_t.inc(ins)
                    mark(f"ow_{grp}_{half}_{kc}", s_t, v)
                    if kc == IC - 1:
                        for tc in tcs:
                            mark(f"owdone_{tc}_{half}", s_t, v)

            for j, tc in enumerate(tcs):
                @block.vector
                def _(vector, grp=grp, half=half, j=j, tc=tc, bb=bb):
                    ii = fin_idx[(grp, half, tc)]
                    if ii == 0:
                        wait(vector, f"hna_{n_nt - 1}_{IC - 1}")
                    wait(vector, f"owdone_{tc}_{half}")
                    w_wait(vector, f"acc_{tc}_{half}")
                    w_wait(vector, f"xr_{tc}_{half}")
                    if ii >= 2:
                        pg, ph, pt = fin_order[ii - 2]
                        saq_wait(vector, f"ost_{pt}_{ph}")
                    accs = af[:, f_acc + (ii % 2) * 512: f_acc + (ii % 2) * 512 + 512]
                    xrs = af[:, f_xrd + (ii % 2) * 512: f_xrd + (ii % 2) * 512 + 512]
                    osts = af[:, f_ost + (ii % 2) * 512: f_ost + (ii % 2) * 512 + 512]
                    vector.drain()
                    vector.scalar_tensor_tensor(osts, ps8[bb + j], 1.0, accs,
                                                ALU.mult, ALU.add)
                    vector.drain()
                    ins = vector.tensor_tensor(osts, osts, xrs, ALU.add)
                    v = s_v.inc(ins)
                    mark(f"ost_{tc}_{half}", s_v, v)
                    mark(f"fin_{tc}_{half}", s_v, v)
                    mark(f"ibk_{tc}_{half}", s_v, v)
                    bank_done(bb + j, f"ibk_{tc}_{half}")


    _emit_dma_queues()
    es.close()
    return nc


w_h12_slot = {}


_orig_build_program = build_program


def build_program(TOK=2048, _orig=_orig_build_program):
    w_h12_slot.clear()
    return _orig(TOK)


def kernel(L0_attn_hidden_states, L0_attn_in_proj_w, L0_attn_conv_w,
           L0_attn_conv_b, L0_attn_ln_g, L0_attn_ln_b, L0_attn_fc_w,
           L0_attn_fc_b, L0_attn_cproj_w, L0_attn_cproj_b, L0_attn_out_w,
           L0_ffn_w1_w, L0_ffn_w1_s, L0_ffn_w2_w, L0_ffn_w2_s,
           L0_ffn_h1_w, L0_ffn_h1_s, L0_ffn_h2_w, L0_ffn_h2_s,
           L0_ffn_w3_w, L0_ffn_w3_s):
    TOK = (B * SEQ) // NCORE
    x = np.ascontiguousarray(np.asarray(L0_attn_hidden_states, dtype=np.float32))
    f32 = lambda a: np.ascontiguousarray(np.asarray(a, dtype=np.float32))
    shared = {
        "inprojT": lhsT_panels(f32(L0_attn_in_proj_w)),
        "convP": conv_panels(f32(L0_attn_conv_w)),
        "fcT": lhsT_panels(f32(L0_attn_fc_w)),
        "cprojT": lhsT_panels(f32(L0_attn_cproj_w)),
        "outwR": rhs_chunks(f32(L0_attn_out_w)),
        "w12T": np.ascontiguousarray(np.stack(
            [lhsT_panels(f32(L0_ffn_w1_w)), lhsT_panels(f32(L0_ffn_w2_w))],
            axis=2)),
        "h1T": lhsT_panels(f32(L0_ffn_h1_w)),
        "h2T": lhsT_panels(f32(L0_ffn_h2_w)),
        "w3R": rhs_chunks(f32(L0_ffn_w3_w)),
        "w3sb": np.ascontiguousarray(
            np.broadcast_to(f32(L0_ffn_w3_s).reshape(1, H), (128, H))),
        "pm": np.concatenate([np.ones((1, 128), np.float32),
                              -np.ones((1, 128), np.float32)], axis=1),
        "onesf": np.ones((128, 1), np.float32),
        "identf": np.eye(128, dtype=np.float32),
        "consts": np.ascontiguousarray(np.broadcast_to(
            np.array([EPS, 0.0, 1.0, -1.0, 0.0, 0.0, 0.0, 0.0], np.float32),
            (128, 8))),
    }
    params = np.zeros((128, 200), np.float32)
    params[:, 0:16] = vec128(f32(L0_attn_conv_b))
    params[:, 16:32] = vec128(f32(L0_attn_ln_g))
    params[:, 32:48] = vec128(f32(L0_attn_ln_b))
    params[:, 48:56] = vec128(f32(L0_attn_fc_b))
    params[:, 56:72] = vec128(f32(L0_attn_cproj_b))
    params[:, 72:104] = vec128(f32(L0_ffn_w1_s))
    params[:, 104:136] = vec128(f32(L0_ffn_w2_s))
    params[:, 136:168] = vec128(f32(L0_ffn_h1_s))
    params[:, 168:200] = vec128(f32(L0_ffn_h2_s))
    shared["params"] = params

    cores_per_b = NCORE // B
    in_maps = []
    for c in range(NCORE):
        b, chunk = divmod(c, cores_per_b)
        s = chunk * TOK
        if chunk == 0:
            halo = np.zeros((128, H), np.float32)
        else:
            halo = x[b, s - 128:s]
        xc = np.ascontiguousarray(np.concatenate([halo, x[b, s:s + TOK]], axis=0))
        m = dict(shared)
        m["x"] = xc
        in_maps.append(m)

    nc = build_program(TOK)
    res = run_bass_kernel_spmd(nc, in_maps, core_ids=list(range(NCORE)))
    out = np.empty((B, SEQ, H), np.float32)
    for c in range(NCORE):
        b, chunk = divmod(c, cores_per_b)
        out[b, chunk * TOK:(chunk + 1) * TOK] = res.results[c]["out"]
    return out


# version 12 (final)
# speedup vs baseline: 1.0382x; 1.0382x over previous
import contextlib

import numpy as np

import concourse.bass as bass
import concourse.mybir as mybir
from concourse.bass_utils import run_bass_kernel_spmd

F32 = mybir.dt.float32
BF16 = mybir.dt.bfloat16
AF = mybir.ActivationFunctionType
ALU = mybir.AluOpType

B = 2
SEQ = 8192
H = 1024
I = 2048
FH = 4096
KTAP = 4
NCORE = 8
HC = H // 128
IC = I // 128
FC = FH // 128
EPS = 1e-5


def lhsT_panels(w):
    M, K = w.shape
    a = w.reshape(M // 128, 128, K // 128, 128)
    return np.ascontiguousarray(a.transpose(0, 3, 2, 1))


def rhs_chunks(w):
    N, K = w.shape
    a = w.reshape(N, K // 128, 128)
    return np.ascontiguousarray(a.transpose(1, 2, 0))


def conv_panels(w):
    a = w.reshape(16, 128, 2, 128, KTAP)
    return np.ascontiguousarray(a.transpose(0, 3, 2, 4, 1))


def vec128(v):
    return np.ascontiguousarray(np.asarray(v).reshape(-1, 128).T)


TRUNC = None


def build_program(TOK=2048):
    trunc = TRUNC
    HALO = 128
    TOKH = TOK + HALO
    HN = TOK + KTAP
    T = min(1024, TOK)
    NI = TOK // T
    NT = min(512, T)
    nt_ffn = T // NT
    n_xtiles = TOKH // 128
    n_nt = TOK // NT
    n_tc = TOK // 128
    n_grp = (n_tc + 7) // 8

    nc = bass.Bass("TRN2", target_bir_lowering=False, debug=False)

    x_d = nc.dram_tensor("x", [TOKH, H], F32, kind="ExternalInput").ap()
    inproj_d = nc.dram_tensor("inprojT", [32, 128, HC, 128], F32, kind="ExternalInput").ap()
    convw_d = nc.dram_tensor("convP", [16, 128, 2, KTAP, 128], F32, kind="ExternalInput").ap()
    fc_d = nc.dram_tensor("fcT", [HC, 128, IC, 128], F32, kind="ExternalInput").ap()
    cproj_d = nc.dram_tensor("cprojT", [IC, 128, HC, 128], F32, kind="ExternalInput").ap()
    outw_d = nc.dram_tensor("outwR", [IC, 128, H], F32, kind="ExternalInput").ap()
    w12_d = nc.dram_tensor("w12T", [FC, 128, 2, HC, 128], F32, kind="ExternalInput").ap()
    h1_d = nc.dram_tensor("h1T", [FC, 128, FC, 128], F32, kind="ExternalInput").ap()
    h2_d = nc.dram_tensor("h2T", [FC, 128, FC, 128], F32, kind="ExternalInput").ap()
    w3_d = nc.dram_tensor("w3R", [FC, 128, H], F32, kind="ExternalInput").ap()
    w3sb_d = nc.dram_tensor("w3sb", [128, H], F32, kind="ExternalInput").ap()
    params_d = nc.dram_tensor("params", [128, 200], F32, kind="ExternalInput").ap()
    consts_d = nc.dram_tensor("consts", [128, 8], F32, kind="ExternalInput").ap()
    pm_d = nc.dram_tensor("pm", [1, 256], F32, kind="ExternalInput").ap()
    onesf_d = nc.dram_tensor("onesf", [128, 1], F32, kind="ExternalInput").ap()
    identf_d = nc.dram_tensor("identf", [128, 128], F32, kind="ExternalInput").ap()
    out_d = nc.dram_tensor("out", [TOK, H], F32, kind="ExternalOutput").ap()
    scr_d = nc.dram_tensor("scr", [TOK, H], F32, kind="Internal").ap()

    es = contextlib.ExitStack()

    RA = 0
    SZ_RA = HC * TOKH
    RB = RA + SZ_RA
    SZ_RB = -(-max(FC * T, IC * HN) // 512) * 512
    RB2 = RB + SZ_RB
    SZ_RB2 = -(-max(FC * T, IC * TOK) // 512) * 512
    RINGS = RB2 + SZ_RB2
    r_w12 = RINGS
    r_h12 = RINGS + 4096
    r_w3 = RINGS + 10240
    r_wsb = RINGS + 12288
    r_sil = RINGS + 13312
    r_sq = RINGS + 14336
    SZ_RINGS = 15360
    r_mxp = RINGS
    r_cvp = RINGS + 2048
    r_fcp = RINGS + 4096
    r_cpp = RINGS + 8192
    r_owp = RINGS + 10240
    r_xnt = RB2
    BF_END = RINGS + SZ_RINGS

    ab = es.enter_context(nc.sbuf_tensor("ab", [128, BF_END], BF16))

    AF32 = 3072
    af = es.enter_context(nc.sbuf_tensor("af", [128, AF32], F32))
    f_xt = 0
    f_stage = 0
    f_rows = 0
    f_v = 1536
    f_acc = 0
    f_xrd = 1024
    f_ost = 2048

    params = es.enter_context(nc.sbuf_tensor("params_s", [128, 200], F32))
    consts = es.enter_context(nc.sbuf_tensor("consts_s", [128, 8], F32))
    pm = es.enter_context(nc.sbuf_tensor("pm_s", [1, 256], F32))
    onesb = es.enter_context(nc.sbuf_tensor("onesb", [128, 1], BF16))
    identb = es.enter_context(nc.sbuf_tensor("identb", [128, 128], BF16))
    statmv = es.enter_context(nc.sbuf_tensor("statmv", [128, 2 * n_xtiles], F32))
    statr = es.enter_context(nc.sbuf_tensor("statr", [128, n_xtiles], F32))
    bn6 = es.enter_context(nc.sbuf_tensor("bn6", [128, 12], F32))

    P_CONVB, P_LNG, P_LNB, P_FCB, P_CPB = 0, 16, 32, 48, 56
    P_W1S, P_W2S, P_H1S, P_H2S = 72, 104, 136, 168

    nc.const_aps.register(F32, EPS, consts[:, 0:1])
    nc.const_aps.register(F32, 0.0, consts[:, 1:2])
    nc.const_aps.register(F32, 1.0, consts[:, 2:3])

    def xn(k, c0, n):
        return ab[:, RA + k * TOKH + c0: RA + k * TOKH + c0 + n]

    def u_t(k, c0, n):
        return ab[:, RA + k * TOK + c0: RA + k * TOK + c0 + n]

    def hh_t(k, c0, n):
        return ab[:, RB + k * T + c0: RB + k * T + c0 + n]

    def g_t(k, c0, n):
        return ab[:, RB2 + k * T + c0: RB2 + k * T + c0 + n]

    def h_t(k, c0, n):
        return ab[:, RB + k * HN + c0: RB + k * HN + c0 + n]

    def hg_t(k, c0, n):
        return ab[:, RB + k * TOK + c0: RB + k * TOK + c0 + n]

    def rc_t(k, c0, n):
        return ab[:, RB2 + k * TOK + c0: RB2 + k * TOK + c0 + n]

    def ring(base, slot, n):
        return ab[:, base + slot * n: base + (slot + 1) * n]

    class Ctr:
        def __init__(self, name):
            self.h = es.enter_context(nc.semaphore(name))
            self.n = 0

        def inc(self, instr, k=1):
            instr.then_inc(self.h, k)
            self.n += k
            return self.n

    NWQ = 8
    NSQ = 4
    NAQ = 4
    s_wq = [Ctr(f"s_w{r}") for r in range(NWQ)]
    s_sq = [Ctr(f"s_s{r}") for r in range(NSQ)]
    s_aq = [Ctr(f"s_a{r}") for r in range(NAQ)]
    s_t = Ctr("s_t")
    s_v = Ctr("s_v")
    s_a = Ctr("s_a")

    block = es.enter_context(nc.Block())

    mk = {}
    _waited = {}

    def _wait_ge(eng, sem, val):
        key = (eng.engine, id(sem))
        if _waited.get(key, -1) >= val:
            return
        _waited[key] = val
        eng.wait_ge(sem, val)

    def mark(name, ctr, val=None):
        mk[name] = (ctr, ctr.n if val is None else val)

    def wait(eng, name):
        ctr, val = mk[name]
        _wait_ge(eng, ctr.h, val)

    wplan = []
    w_index = {}

    def wadd(key, dst, src, start_el, n_el):
        w_index[key] = len(wplan)
        wplan.append((key, dst, src, start_el, start_el + n_el))

    def w_wait(eng, key):
        idx = w_index[key]
        _wait_ge(eng, s_wq[idx % NWQ].h, 16 * (idx // NWQ + 1))

    wadd("ones", onesb[:, :], onesf_d, -1, 0)
    wadd("ident", identb[:, :], identf_d, -2, 0)
    wadd("w3sb", ab[:, r_wsb:r_wsb + H], w3sb_d, r_wsb, H)
    h12_ctr = [0]
    for it in range(NI):
        for m in range(FC):
            sl = m % 2
            wadd(f"w12_{it}_{m}", ring(r_w12, sl, 2048), w12_d[m],
                 r_w12 + sl * 2048, 2048)
        for hw, hd in (("h1", h1_d), ("h2", h2_d)):
            for m in range(FC):
                for hf in range(2):
                    sl = h12_ctr[0] % 3
                    h12_ctr[0] += 1
                    w_h12_slot[(hw, it, m, hf)] = sl
                    wadd(f"{hw}_{it}_{m}_{hf}", ring(r_h12, sl, 2048),
                         hd[m, :, hf * 16:(hf + 1) * 16, :],
                         r_h12 + sl * 2048, 2048)
        for half in range(2):
            for kc in range(FC):
                sl = kc % 4
                wadd(f"w3_{it}_{half}_{kc}", ring(r_w3, sl, 512),
                     w3_d[kc, :, half * 512:(half + 1) * 512],
                     r_w3 + sl * 512, 512)
    if trunc not in ("B",):
        for m in range(IC):
            wadd(f"iph_{m}", ring(r_mxp, m % 2, 1024), inproj_d[m],
                 r_mxp + (m % 2) * 1024, 1024)
        for oc in range(16):
            wadd(f"cv_{oc}", ring(r_cvp, oc % 2, 1024), convw_d[oc],
                 r_cvp + (oc % 2) * 1024, 1024)
        for m in range(IC):
            wadd(f"ipg_{m}", ring(r_mxp, m % 2, 1024), inproj_d[IC + m],
                 r_mxp + (m % 2) * 1024, 1024)
    if trunc not in ("B", "E"):
        for nt in range(n_nt):
            for m in range(HC):
                wadd(f"fc_{nt}_{m}", ring(r_fcp, m % 2, 2048), fc_d[m],
                     r_fcp + (m % 2) * 2048, 2048)
            for m in range(IC):
                wadd(f"cp_{nt}_{m}", ring(r_cpp, m % 2, 1024), cproj_d[m],
                     r_cpp + (m % 2) * 1024, 1024)
    if trunc is None:
        _fin = []
        for grp in range(n_grp):
            for half in range(2):
                for tc in range(grp * 8, min((grp + 1) * 8, n_tc)):
                    _fin.append((grp, half, tc))
        for grp in range(n_grp):
            for half in range(2):
                for kc in range(IC):
                    sl = kc % 4
                    wadd(f"ow_{grp}_{half}_{kc}", ring(r_owp, sl, 512),
                         outw_d[kc, :, half * 512:(half + 1) * 512],
                         r_owp + sl * 512, 512)
                for tc in range(grp * 8, min((grp + 1) * 8, n_tc)):
                    ii = _fin.index((grp, half, tc))
                    wadd(f"acc_{tc}_{half}",
                         af[:, f_acc + (ii % 2) * 512: f_acc + (ii % 2) * 512 + 512],
                         scr_d[tc * 128:(tc + 1) * 128,
                               half * 512:(half + 1) * 512], -1, 0)
                    wadd(f"xr_{tc}_{half}",
                         af[:, f_xrd + (ii % 2) * 512: f_xrd + (ii % 2) * 512 + 512],
                         x_d[HALO + tc * 128: HALO + (tc + 1) * 128,
                             half * 512:(half + 1) * 512], -1, 0)

    splan = []
    s_qidx = {}

    def sadd(key, dst, src, kind="load", data_mark=None):
        s_qidx[key] = len(splan)
        splan.append((key, kind, dst, src, data_mark))

    def sload_wait(eng, key):
        idx = s_qidx[key]
        _wait_ge(eng, s_sq[idx % NSQ].h, 16 * (idx // NSQ + 1))

    sadd("params", params[:, :], params_d)
    sadd("consts", consts[:, :], consts_d)
    sadd("pm", pm[:, :], pm_d)
    for i in range(n_xtiles):
        sadd(f"xt{i}",
             af[:, f_xt + (i % 2) * 1024: f_xt + (i % 2) * 1024 + H],
             x_d[i * 128:(i + 1) * 128, :])

    def store_emit(scalar, key, dst, src, data_mark):
        if key not in s_qidx:
            sadd(key, dst, src, kind="store", data_mark=data_mark)

    def saq_wait(eng, key):
        idx = s_qidx[key]
        _wait_ge(eng, s_sq[idx % NSQ].h, 16 * (idx // NSQ + 1))

    def saq_wait_all(eng, key):
        idx = s_qidx[key]
        for r in range(NSQ):
            cnt = (idx - r) // NSQ + 1 if idx >= r else 0
            if cnt > 0:
                _wait_ge(eng, s_sq[r].h, 16 * cnt)

    def emit_tile(i):
        @block.vector
        def _(vector, i=i):
            sload_wait(vector, f"xt{i}")
            xt = af[:, f_xt + (i % 2) * 1024: f_xt + (i % 2) * 1024 + H]
            vector.drain()
            vector.bn_stats(bn6[:, 0:6], xt[:, 0:512])
            vector.bn_stats(bn6[:, 6:12], xt[:, 512:1024])
            vector.drain()
            ins = vector.bn_aggr(statmv[:, 2 * i:2 * i + 2], bn6[:, :])
            mark(f"stats{i}", s_v, s_v.inc(ins))

        @block.scalar
        def _(scalar, i=i):
            if i == 0:
                sload_wait(scalar, "consts")
            wait(scalar, f"stats{i}")
            scalar.drain()
            scalar.activation(statr[:, i:i + 1], statmv[:, 2 * i + 1:2 * i + 2],
                              AF.Ln, bias=EPS, scale=1.0)
            scalar.drain()
            ins = scalar.activation(statr[:, i:i + 1], statr[:, i:i + 1],
                                    AF.Exp, bias=0.0, scale=-0.5)
            mark(f"rsq{i}", s_a, s_a.inc(ins))

        @block.vector
        def _(vector, i=i):
            wait(vector, f"rsq{i}")
            if i >= 2:
                wait(vector, f"tr{i - 2}")
            xt = af[:, f_xt + (i % 2) * 1024: f_xt + (i % 2) * 1024 + H]
            vector.drain()
            ins = vector.tensor_scalar(ring(r_xnt, i % 2, 1024), xt,
                                       statmv[:, 2 * i:2 * i + 1],
                                       statr[:, i:i + 1],
                                       ALU.subtract, ALU.mult)
            mark(f"apply{i}", s_v, s_v.inc(ins))
            mark(f"xt{i}", s_v)

        @block.tensor
        def _(tensor, i=i):
            if i == 0:
                w_wait(tensor, "ident")
            wait(tensor, f"apply{i}")
            if i >= 1:
                wait(tensor, f"evac{i - 1}")
            xnt = ring(r_xnt, i % 2, 1024)
            for k in range(HC):
                ins = tensor.transpose(ps_tr[:, k * 128:(k + 1) * 128],
                                       xnt[:, k * 128:(k + 1) * 128],
                                       identb[:, :])
            mark(f"tr{i}", s_t, s_t.inc(ins))

        @block.vector
        def _(vector, i=i):
            wait(vector, f"tr{i}")
            for k in range(HC):
                ins = vector.tensor_copy(xn(k, i * 128, 128),
                                         ps_tr[:, k * 128:(k + 1) * 128])
            mark(f"evac{i}", s_v, s_v.inc(ins))

    ps_mm = [es.enter_context(nc.psum_tensor(f"ps_mm{j}", [128, 512], F32))
             for j in range(4)]
    ps_tok = [es.enter_context(nc.psum_tensor("ps_tok0", [128, 1024], F32))]
    es_tr = contextlib.ExitStack()
    ps_tr = es_tr.enter_context(nc.psum_tensor("ps_tr", [128, 1024], BF16))

    upfront = min((HALO + T - 1) // 128, n_xtiles - 1)
    for i in range(upfront + 1):
        emit_tile(i)
    pend_tiles = list(range(upfront + 1, n_xtiles))

    XN_READY = f"evac{n_xtiles - 1}"

    bank_mk = {}

    def bank_wait(tensor, bank):
        if bank in bank_mk:
            wait(tensor, bank_mk[bank])

    def bank_done(bank, name):
        bank_mk[bank] = name

    ps8 = None

    spill_order = []

    for it in range(NI):
        c0 = HALO + it * T

        for m in range(FC):
            if it == 0 and m % 3 == 0 and pend_tiles:
                emit_tile(pend_tiles.pop(0))

            @block.tensor
            def _(tensor, m=m, it=it, c0=c0):
                w_wait(tensor, f"w12_{it}_{m}")
                for half in range(2):
                    pan = ring(r_w12, m % 2, 2048)[:, half * 1024:(half + 1) * 1024]
                    pan3 = pan.rearrange("p (kc m) -> p kc m", kc=HC)
                    for t2 in range(nt_ffn):
                        if m == 0:
                            need = (c0 + t2 * NT + NT - 1) // 128
                            wait(tensor, f"evac{need}")
                        bank = 2 * half + t2
                        bank_wait(tensor, bank)
                        for k in range(HC):
                            ins = tensor.matmul(
                                ps_mm[bank][:, 0:NT], pan3[:, k, :],
                                xn(k, c0 + t2 * NT, NT),
                                start=(k == 0), stop=(k == HC - 1))
                    mark(f"w{half + 1}_{it}_{m}", s_t, s_t.inc(ins))
                mark(f"w12_{it}_{m}", s_t, s_t.n)

            @block.scalar
            def _(scalar, m=m, it=it):
                if it == 0 and m == 0:
                    sload_wait(scalar, "params")
                wait(scalar, f"w1_{it}_{m}")
                for t2 in range(nt_ffn):
                    if f"silfree_{t2}" in mk:
                        wait(scalar, f"silfree_{t2}")
                    ins = scalar.activation(
                        ring(r_sil, t2, 512)[:, 0:NT], ps_mm[t2][:, 0:NT],
                        AF.Silu, bias=0.0,
                        scale=params[:, P_W1S + m:P_W1S + m + 1])
                    mark(f"sil_{t2}", s_a, s_a.inc(ins))

            @block.vector
            def _(vector, m=m, it=it):
                if it == 0 and m == 0:
                    sload_wait(vector, "params")
                wait(vector, f"w2_{it}_{m}")
                for t2 in range(nt_ffn):
                    wait(vector, f"sil_{t2}")
                    vector.drain()
                    ins = vector.scalar_tensor_tensor(
                        hh_t(m, t2 * NT, NT), ps_mm[2 + t2][:, 0:NT],
                        params[:, P_W2S + m:P_W2S + m + 1],
                        ring(r_sil, t2, 512)[:, 0:NT], ALU.mult, ALU.mult)
                    v = s_v.inc(ins)
                    mark(f"silfree_{t2}", s_v, v)
                    bank_done(t2, f"silfree_{t2}")
                    mark(f"w2bk_{it}_{m}_{t2}", s_v, v)
                    bank_done(2 + t2, f"w2bk_{it}_{m}_{t2}")
                mark(f"hh_{it}_{m}", s_v, s_v.n)

        if it == 0:
            while pend_tiles:
                emit_tile(pend_tiles.pop(0))
            es_tr.close()
            ps_tok.append(es.enter_context(
                nc.psum_tensor("ps_tok1", [128, 1024], F32)))
            ps8 = [t[:, 0:512] for t in ps_mm] + [
                ps_tok[0][:, 0:512], ps_tok[0][:, 512:1024],
                ps_tok[1][:, 0:512], ps_tok[1][:, 512:1024]]
            bank_mk[6] = XN_READY
            bank_mk[7] = XN_READY

        for hw, psc, src_t, dst_t in (("h1", P_H1S, hh_t, g_t),
                                      ("h2", P_H2S, g_t, hh_t)):
            for m in range(FC):
                @block.tensor
                def _(tensor, m=m, it=it, hw=hw, src_t=src_t):
                    if m == 0:
                        wait(tensor, (f"hh_{it}_{FC - 1}" if hw == "h1"
                                      else f"gg_{it}_{FC - 1}"))
                    for t2 in range(nt_ffn):
                        bank = 2 * (m % 2) + t2
                        bank_wait(tensor, bank)
                        for hf in range(2):
                            w_wait(tensor, f"{hw}_{it}_{m}_{hf}")
                            pan = ring(r_h12, w_h12_slot[(hw, it, m, hf)], 2048)
                            pan3 = pan.rearrange("p (kc m) -> p kc m", kc=16)
                            for kk in range(16):
                                k = hf * 16 + kk
                                ins = tensor.matmul(
                                    ps_mm[bank][:, 0:NT], pan3[:, kk, :],
                                    src_t(k, t2 * NT, NT),
                                    start=(k == 0), stop=(k == FC - 1))
                            v = s_t.inc(ins)
                            if t2 == nt_ffn - 1:
                                mark(f"{hw}_{it}_{m}_{hf}", s_t, v)
                        mark(f"{hw}mm_{it}_{m}_{t2}", s_t, v)

                @block.scalar
                def _(scalar, m=m, it=it, hw=hw, psc=psc, dst_t=dst_t):
                    for t2 in range(nt_ffn):
                        bank = 2 * (m % 2) + t2
                        wait(scalar, f"{hw}mm_{it}_{m}_{t2}")
                        ins = scalar.activation(
                            dst_t(m, t2 * NT, NT), ps_mm[bank][:, 0:NT],
                            AF.Gelu, bias=0.0,
                            scale=params[:, psc + m:psc + m + 1])
                        a = s_a.inc(ins)
                        mark(f"{hw}e_{it}_{m}_{t2}", s_a, a)
                        bank_done(bank, f"{hw}e_{it}_{m}_{t2}")
                    key = "gg" if hw == "h1" else "g2"
                    mark(f"{key}_{it}_{m}", s_a, s_a.n)

        n_tch = T // 128
        for half in range(2):
            for kc in range(FC):
                @block.vector
                def _(vector, it=it, half=half, kc=kc):
                    w_wait(vector, f"w3_{it}_{half}_{kc}")
                    li = (it * 2 + half) * FC + kc
                    if li >= 4:
                        pit, rem = divmod(li - 4, 2 * FC)
                        phf, pkc = divmod(rem, FC)
                        wait(vector, f"w3c_{pit}_{phf}_{pkc}")
                    if it == 0 and half == 0 and kc == 0:
                        w_wait(vector, "w3sb")
                    pan = ring(r_w3, kc % 4, 512)
                    vector.drain()
                    ins = vector.tensor_tensor(
                        pan, pan,
                        ab[:, r_wsb + half * 512: r_wsb + (half + 1) * 512],
                        ALU.mult)
                    mark(f"w3s_{it}_{half}_{kc}", s_v, s_v.inc(ins))

                @block.tensor
                def _(tensor, it=it, half=half, kc=kc, n_tch=n_tch):
                    if half == 0 and kc == 0:
                        wait(tensor, f"g2_{it}_{FC - 1}")
                    wait(tensor, f"w3s_{it}_{half}_{kc}")
                    pan = ring(r_w3, kc % 4, 512)
                    for tc in range(n_tch):
                        if kc == 0:
                            bank_wait(tensor, tc)
                        ins = tensor.matmul(
                            ps8[tc], hh_t(kc, tc * 128, 128), pan,
                            start=(kc == 0), stop=(kc == FC - 1))
                    v = s_t.inc(ins)
                    mark(f"w3c_{it}_{half}_{kc}", s_t, v)
                    mark(f"w3_{it}_{half}_{kc}", s_t, v)
                    if kc == FC - 1:
                        mark(f"w3done_{it}_{half}", s_t, v)

            for tc in range(n_tch):
                spill_order.append(f"spill_{it}_{half}_{tc}")

            for tc in range(n_tch):
                @block.vector
                def _(vector, it=it, half=half, tc=tc, n_tch=n_tch):
                    if tc == 0:
                        wait(vector, f"w3done_{it}_{half}")
                    li = (it * 2 + half) * n_tch + tc
                    slot = li % 4
                    if li >= 4:
                        saq_wait(vector, spill_order[li - 4])
                    st = af[:, f_stage + slot * 512: f_stage + slot * 512 + 512]
                    vector.drain()
                    ins = vector.tensor_copy(st, ps8[tc])
                    v = s_v.inc(ins)
                    mark(f"spill_{it}_{half}_{tc}", s_v, v)
                    mark(f"w3bk_{it}_{half}_{tc}", s_v, v)
                    bank_done(tc, f"w3bk_{it}_{half}_{tc}")

                @block.scalar
                def _(scalar, it=it, half=half, tc=tc, n_tch=n_tch):
                    li = (it * 2 + half) * n_tch + tc
                    slot = li % 4
                    st = af[:, f_stage + slot * 512: f_stage + slot * 512 + 512]
                    store_emit(scalar, f"spill_{it}_{half}_{tc}",
                               scr_d[it * T + tc * 128: it * T + (tc + 1) * 128,
                                     half * 512:(half + 1) * 512],
                               st, f"spill_{it}_{half}_{tc}")

    LAST_SPILL = spill_order[-1]

    def _emit_dma_queues():
        @block.gpsimd
        def _(gpsimd):
            lastblk = {}
            for idx, (key, dst, src, st_el, en_el) in enumerate(wplan):
                r = idx % NWQ
                if idx >= NWQ:
                    gpsimd.wait_ge(s_wq[r].h, 16 * (idx // NWQ))
                if key.startswith("acc_") or key.startswith("xr_"):
                    pre, tc2, half2 = key.split("_")
                    tc2, half2 = int(tc2), int(half2)
                    ii = fin_idx[(tc2 // 8, half2, tc2)]
                    if ii == 0 and pre == "acc":
                        wait(gpsimd, f"bc_{n_nt - 1}")
                        saq_wait_all(gpsimd, LAST_SPILL)
                    if ii == 0 and pre == "xr":
                        wait(gpsimd, f"hna_{n_nt - 1}_{IC - 1}")
                    if ii >= 2:
                        pg, ph, pt = fin_order[ii - 2]
                        wait(gpsimd, f"fin_{pt}_{ph}")
                if st_el >= 0:
                    waits = {lastblk[b] for b in range(st_el // 512,
                                                       (en_el + 511) // 512)
                             if b in lastblk}
                    for wkey in waits:
                        wait(gpsimd, wkey)
                ins = gpsimd.dma_start(out=dst, in_=src)
                s_wq[r].inc(ins, 16)
                if st_el >= 0:
                    for bk in range(st_el // 512, (en_el + 511) // 512):
                        lastblk[bk] = key

        @block.sync
        def _(sync):
            for idx, (key, kind, dst, src, data_mark) in enumerate(splan):
                r = idx % NSQ
                if idx >= NSQ:
                    sync.wait_ge(s_sq[r].h, 16 * (idx // NSQ))
                if kind == "store":
                    wait(sync, data_mark)
                elif key.startswith("xt"):
                    i = int(key[2:])
                    if i >= 2:
                        wait(sync, f"xt{i - 2}")
                ins = sync.dma_start(out=dst, in_=src)
                s_sq[r].inc(ins, 16)
            for r in range(NSQ):
                sync.wait_ge(s_sq[r].h, s_sq[r].n)

    def _finish_truncated():
        _emit_dma_queues()
        es.close()
        return nc

    if trunc == "B":
        return _finish_truncated()

    h_tiles = []
    off = 0
    while off < HN:
        nn = min(512, HN - off)
        h_tiles.append((off, nn))
        off += nn

    for m in range(IC):
        @block.tensor
        def _(tensor, m=m):
            w_wait(tensor, f"iph_{m}")
            pan = ring(r_mxp, m % 2, 1024)
            pan3 = pan.rearrange("p (kc m) -> p kc m", kc=HC)
            for ti, (off, nn) in enumerate(h_tiles):
                bank_wait(tensor, ti)
                for k in range(HC):
                    ins = tensor.matmul(ps8[ti][:, 0:nn], pan3[:, k, :],
                                        xn(k, 124 + off, nn),
                                        start=(k == 0), stop=(k == HC - 1))
                mark(f"iphm_{m}_{ti}", s_t, s_t.inc(ins))
            mark(f"iph_{m}", s_t, s_t.n)

        if m % 2 == 0:
            @block.vector
            def _(vector, m=m):
                for ti, (off, nn) in enumerate(h_tiles):
                    wait(vector, f"iphm_{m}_{ti}")
                    vector.drain()
                    ins = vector.tensor_copy(h_t(m, off, nn),
                                             ps8[ti][:, 0:nn])
                    mark(f"he_{m}_{ti}", s_v, s_v.inc(ins))
                    bank_done(ti, f"he_{m}_{ti}")
                mark(f"h_{m}", s_v, s_v.n)
        else:
            @block.scalar
            def _(scalar, m=m):
                for ti, (off, nn) in enumerate(h_tiles):
                    wait(scalar, f"iphm_{m}_{ti}")
                    ins = scalar.activation(h_t(m, off, nn),
                                            ps8[ti][:, 0:nn],
                                            AF.Copy, bias=0.0, scale=1.0)
                    mark(f"he_{m}_{ti}", s_a, s_a.inc(ins))
                    bank_done(ti, f"he_{m}_{ti}")
                mark(f"h_{m}", s_a, s_a.n)

    for oc in range(16):
        @block.tensor
        def _(tensor, oc=oc):
            w_wait(tensor, f"cv_{oc}")
            g2c = 2 * (oc // 2)
            wait(tensor, f"h_{g2c}")
            wait(tensor, f"h_{g2c + 1}")
            pan = ring(r_cvp, oc % 2, 1024)
            pan4 = pan.rearrange("p (ic k m) -> p ic k m", ic=2, k=KTAP)
            for nt in range(n_nt):
                bank = (oc * n_nt + nt) % 4
                bank_wait(tensor, bank)
                first = True
                for icl in range(2):
                    for k in range(KTAP):
                        ins = tensor.matmul(
                            ps_mm[bank][:, 0:NT], pan4[:, icl, k, :],
                            h_t(g2c + icl, nt * NT + 1 + k, NT),
                            start=first, stop=(icl == 1 and k == KTAP - 1))
                        first = False
                mark(f"cvm_{oc}_{nt}", s_t, s_t.inc(ins))
            mark(f"cv_{oc}", s_t, s_t.n)

        @block.vector
        def _(vector, oc=oc):
            for nt in range(n_nt):
                bank = (oc * n_nt + nt) % 4
                wait(vector, f"cvm_{oc}_{nt}")
                vector.drain()
                ins = vector.tensor_scalar(
                    rc_t(oc, nt * NT, NT), ps_mm[bank][:, 0:NT],
                    params[:, P_CONVB + oc:P_CONVB + oc + 1], 0.0,
                    ALU.add, ALU.add)
                mark(f"cve_{oc}_{nt}", s_v, s_v.inc(ins))
                bank_done(bank, f"cve_{oc}_{nt}")
            mark(f"cvo_{oc}", s_v, s_v.n)

    for m in range(IC):
        for nt in range(n_nt):
            @block.tensor
            def _(tensor, m=m, nt=nt):
                if nt == 0:
                    w_wait(tensor, f"ipg_{m}")
                bank = (m * n_nt + nt) % 4
                bank_wait(tensor, bank)
                pan = ring(r_mxp, m % 2, 1024)
                pan3 = pan.rearrange("p (kc m) -> p kc m", kc=HC)
                for k in range(HC):
                    ins = tensor.matmul(ps_mm[bank][:, 0:NT], pan3[:, k, :],
                                        xn(k, HALO + nt * NT, NT),
                                        start=(k == 0), stop=(k == HC - 1))
                mark(f"ipgm_{m}_{nt}", s_t, s_t.inc(ins))
                if nt == n_nt - 1:
                    mark(f"ipg_{m}", s_t, s_t.n)

            @block.scalar
            def _(scalar, m=m, nt=nt):
                bank = (m * n_nt + nt) % 4
                wait(scalar, f"ipgm_{m}_{nt}")
                sl = (m * n_nt + nt) % 2
                if f"gsilfree_{sl}" in mk:
                    wait(scalar, f"gsilfree_{sl}")
                ins = scalar.activation(ring(r_sil, sl, 512)[:, 0:NT],
                                        ps_mm[bank][:, 0:NT], AF.Silu,
                                        bias=0.0, scale=1.0)
                mark(f"gsil_{m}_{nt}", s_a, s_a.inc(ins))

            @block.vector
            def _(vector, m=m, nt=nt):
                bank = (m * n_nt + nt) % 4
                sl = (m * n_nt + nt) % 2
                wait(vector, f"gsil_{m}_{nt}")
                wait(vector, f"cvo_{m}")
                vector.drain()
                ins = vector.tensor_tensor(hg_t(m, nt * NT, NT),
                                           ring(r_sil, sl, 512)[:, 0:NT],
                                           rc_t(m, nt * NT, NT), ALU.mult)
                v = s_v.inc(ins)
                mark(f"gsilfree_{sl}", s_v, v)
                mark(f"hge_{m}_{nt}", s_v, v)
                bank_done(bank, f"hge_{m}_{nt}")
                if nt == n_nt - 1:
                    mark(f"hg_{m}", s_v, s_v.n)

    if trunc == "E":
        return _finish_truncated()

    inv_n = 1.0 / float(I)

    def emit_fccp(nt):
        for m in range(HC):
            @block.tensor
            def _(tensor, nt=nt, m=m):
                w_wait(tensor, f"fc_{nt}_{m}")
                if m == 0:
                    wait(tensor, f"hna_{nt}_{IC - 1}")
                pan = ring(r_fcp, m % 2, 2048)
                pan3 = pan.rearrange("p (kc m) -> p kc m", kc=IC)
                bank = m % 4
                bank_wait(tensor, bank)
                for k in range(IC):
                    ins = tensor.matmul(ps_mm[bank][:, 0:NT], pan3[:, k, :],
                                        rc_t(k, nt * NT, NT),
                                        start=(k == 0), stop=(k == IC - 1))
                mark(f"fcm_{nt}_{m}", s_t, s_t.inc(ins))
                mark(f"fc_{nt}_{m}", s_t, s_t.n)

            @block.scalar
            def _(scalar, nt=nt, m=m):
                bank = m % 4
                wait(scalar, f"fcm_{nt}_{m}")
                ins = scalar.activation(u_t(m, nt * NT, NT),
                                        ps_mm[bank][:, 0:NT], AF.Silu,
                                        bias=params[:, P_FCB + m:P_FCB + m + 1],
                                        scale=1.0)
                mark(f"ue_{nt}_{m}", s_a, s_a.inc(ins))
                bank_done(bank, f"ue_{nt}_{m}")

        for m in range(IC):
            @block.tensor
            def _(tensor, nt=nt, m=m):
                w_wait(tensor, f"cp_{nt}_{m}")
                if m == 0:
                    wait(tensor, f"ue_{nt}_{HC - 1}")
                pan = ring(r_cpp, m % 2, 1024)
                pan3 = pan.rearrange("p (kc m) -> p kc m", kc=HC)
                bank = m % 4
                bank_wait(tensor, bank)
                for k in range(HC):
                    ins = tensor.matmul(ps_mm[bank][:, 0:NT], pan3[:, k, :],
                                        u_t(k, nt * NT, NT),
                                        start=(k == 0), stop=(k == HC - 1))
                mark(f"cpm_{nt}_{m}", s_t, s_t.inc(ins))
                mark(f"cp_{nt}_{m}", s_t, s_t.n)

            @block.vector
            def _(vector, nt=nt, m=m):
                bank = m % 4
                wait(vector, f"cpm_{nt}_{m}")
                vector.drain()
                ins = vector.scalar_tensor_tensor(
                    rc_t(m, nt * NT, NT), ps_mm[bank][:, 0:NT],
                    params[:, P_CPB + m:P_CPB + m + 1], hg_t(m, nt * NT, NT),
                    ALU.add, ALU.add)
                mark(f"at2e_{nt}_{m}", s_v, s_v.inc(ins))
                bank_done(bank, f"at2e_{nt}_{m}")
        mark(f"at2nt_{nt}", s_v, s_v.n)

    def emit_apply(nt):
        p = nt % 2
        for kc in range(IC):
            @block.vector
            def _(vector, nt=nt, kc=kc, p=p):
                if kc == 0:
                    wait(vector, f"bc_{nt}")
                vs = af[:, f_v + (kc % 2) * 512: f_v + (kc % 2) * 512 + NT]
                li = nt * IC + kc
                if li >= 2:
                    pnt, pkc = divmod(li - 2, IC)
                    wait(vector, f"hna_{pnt}_{pkc}")
                vector.drain()
                vector.tensor_tensor(vs, hg_t(kc, nt * NT, NT),
                                     ps_tok[p][:, 0:NT], ALU.mult)
                vector.drain()
                ins = vector.tensor_tensor(vs, vs, ps_tok[p][:, 512:512 + NT],
                                           ALU.add)
                v = s_v.inc(ins)
                mark(f"vs_{nt}_{kc}", s_v, v)
                if kc == IC - 1:
                    mark(f"tokbk_{nt}", s_v, v)
                    bank_done(4 + 2 * p, f"tokbk_{nt}")
                    bank_done(5 + 2 * p, f"tokbk_{nt}")

            @block.scalar
            def _(scalar, nt=nt, kc=kc):
                wait(scalar, f"vs_{nt}_{kc}")
                vs = af[:, f_v + (kc % 2) * 512: f_v + (kc % 2) * 512 + NT]
                ins = scalar.activation(rc_t(kc, nt * NT, NT), vs, AF.Identity,
                                        bias=params[:, P_LNB + kc:P_LNB + kc + 1],
                                        scale=params[:, P_LNG + kc:P_LNG + kc + 1])
                mark(f"hna_{nt}_{kc}", s_a, s_a.inc(ins))

    for nt in range(n_nt):
        sb, qb = 2 * (nt % 2), 2 * (nt % 2) + 1
        p = nt % 2

        for kc in range(IC):
            @block.vector
            def _(vector, nt=nt, kc=kc):
                if nt == 0:
                    wait(vector, f"hg_{kc}")
                li = nt * IC + kc
                if li >= 2:
                    pnt, pkc = divmod(li - 2, IC)
                    wait(vector, f"sqc_{pnt}_{pkc}")
                vector.drain()
                ins = vector.tensor_tensor(ring(r_sq, kc % 2, 512)[:, 0:NT],
                                           hg_t(kc, nt * NT, NT),
                                           hg_t(kc, nt * NT, NT), ALU.mult)
                mark(f"sq_{nt}_{kc}", s_v, s_v.inc(ins))

            @block.tensor
            def _(tensor, nt=nt, kc=kc, sb=sb, qb=qb):
                if kc == 0:
                    if nt == 0:
                        w_wait(tensor, "ones")
                    bank_wait(tensor, sb)
                    bank_wait(tensor, qb)
                wait(tensor, f"hg_{kc}")
                tensor.matmul(ps_mm[sb][0:1, 0:NT], onesb[:, :],
                              hg_t(kc, nt * NT, NT),
                              start=(kc == 0), stop=(kc == IC - 1))
                wait(tensor, f"sq_{nt}_{kc}")
                ins = tensor.matmul(ps_mm[qb][0:1, 0:NT], onesb[:, :],
                                    ring(r_sq, kc % 2, 512)[:, 0:NT],
                                    start=(kc == 0), stop=(kc == IC - 1))
                mark(f"sqc_{nt}_{kc}", s_t, s_t.inc(ins))
                if kc == IC - 1:
                    mark(f"st2_{nt}", s_t, s_t.n)

        mean = af[0:1, f_rows + 0: f_rows + NT]
        rr = af[0:1, f_rows + 512: f_rows + 512 + NT]
        mr = af[0:1, f_rows + 1024: f_rows + 1024 + NT]

        @block.vector
        def _(vector, nt=nt, sb=sb, qb=qb, mean=mean, rr=rr, mr=mr):
            wait(vector, f"st2_{nt}")
            if nt == 0:
                saq_wait_all(vector, LAST_SPILL)
            if nt >= 1:
                wait(vector, f"bc_{nt - 1}")
            vector.drain()
            vector.tensor_scalar(mean, ps_mm[sb][0:1, 0:NT], inv_n, 0.0,
                                 ALU.mult, ALU.add)
            vector.drain()
            vector.tensor_tensor(mr, mean, mean, ALU.mult)
            vector.drain()
            ins = vector.scalar_tensor_tensor(rr, ps_mm[qb][0:1, 0:NT], inv_n,
                                              mr, ALU.mult, ALU.subtract)
            mark(f"var_{nt}", s_v, s_v.inc(ins))
            mark(f"stat_sb_{nt}", s_v, s_v.n)
            bank_done(sb, f"stat_sb_{nt}")

        @block.scalar
        def _(scalar, nt=nt, rr=rr):
            wait(scalar, f"var_{nt}")
            scalar.drain()
            scalar.activation(rr, rr, AF.Ln, bias=EPS, scale=1.0)
            scalar.drain()
            ins = scalar.activation(rr, rr, AF.Exp, bias=0.0, scale=-0.5)
            mark(f"r_{nt}", s_a, s_a.inc(ins))

        @block.vector
        def _(vector, nt=nt, qb=qb, mean=mean, rr=rr, mr=mr):
            wait(vector, f"r_{nt}")
            vector.drain()
            ins = vector.tensor_tensor(mr, mean, rr, ALU.mult)
            mark(f"rows_{nt}", s_v, s_v.inc(ins))
            bank_done(qb, f"rows_{nt}")

        @block.tensor
        def _(tensor, nt=nt, p=p, rr=rr, mr=mr):
            if nt == 0:
                sload_wait(tensor, "pm")
            wait(tensor, f"rows_{nt}")
            bank_wait(tensor, 4 + 2 * p)
            bank_wait(tensor, 5 + 2 * p)
            tensor.matmul(ps_tok[p][:, 0:NT], pm[:, 0:128], rr,
                          start=True, stop=True)
            ins = tensor.matmul(ps_tok[p][:, 512:512 + NT], pm[:, 128:256], mr,
                                start=True, stop=True)
            mark(f"bc_{nt}", s_t, s_t.inc(ins))

        if nt >= 1:
            emit_apply(nt - 1)
        if nt >= 2:
            emit_fccp(nt - 2)

    emit_apply(n_nt - 1)
    if n_nt >= 2:
        emit_fccp(n_nt - 2)
    emit_fccp(n_nt - 1)

    if trunc == "H":
        return _finish_truncated()

    fin_order = []
    for grp in range(n_grp):
        for half in range(2):
            for tc in range(grp * 8, min((grp + 1) * 8, n_tc)):
                fin_order.append((grp, half, tc))
    fin_idx = {t: i for i, t in enumerate(fin_order)}

    for ii, (grp, half, tc) in enumerate(fin_order):
        sadd(f"ost_{tc}_{half}",
             out_d[tc * 128:(tc + 1) * 128, half * 512:(half + 1) * 512],
             af[:, f_ost + (ii % 2) * 512: f_ost + (ii % 2) * 512 + 512],
             kind="store", data_mark=f"ost_{tc}_{half}")

    for grp in range(n_grp):
        tcs = list(range(grp * 8, min((grp + 1) * 8, n_tc)))
        need_nt = (tcs[-1] * 128 + 127) // NT
        for half in range(2):
            @block.tensor
            def _(tensor, grp=grp, half=half, tcs=tcs, need_nt=need_nt):
                if half == 0:
                    wait(tensor, f"at2nt_{need_nt}")
                for kc in range(IC):
                    w_wait(tensor, f"ow_{grp}_{half}_{kc}")
                    pan = ring(r_owp, kc % 4, 512)
                    for j, tc in enumerate(tcs):
                        if kc == 0:
                            bank_wait(tensor, j)
                        ins = tensor.matmul(
                            ps8[j], rc_t(kc, tc * 128, 128), pan,
                            start=(kc == 0), stop=(kc == IC - 1))
                    v = s_t.inc(ins)
                    mark(f"ow_{grp}_{half}_{kc}", s_t, v)
                    if kc == IC - 1:
                        for tc in tcs:
                            mark(f"owdone_{tc}_{half}", s_t, v)

            for j, tc in enumerate(tcs):
                @block.vector
                def _(vector, grp=grp, half=half, j=j, tc=tc):
                    ii = fin_idx[(grp, half, tc)]
                    if ii == 0:
                        wait(vector, f"hna_{n_nt - 1}_{IC - 1}")
                    wait(vector, f"owdone_{tc}_{half}")
                    w_wait(vector, f"acc_{tc}_{half}")
                    w_wait(vector, f"xr_{tc}_{half}")
                    if ii >= 2:
                        pg, ph, pt = fin_order[ii - 2]
                        saq_wait(vector, f"ost_{pt}_{ph}")
                    accs = af[:, f_acc + (ii % 2) * 512: f_acc + (ii % 2) * 512 + 512]
                    xrs = af[:, f_xrd + (ii % 2) * 512: f_xrd + (ii % 2) * 512 + 512]
                    osts = af[:, f_ost + (ii % 2) * 512: f_ost + (ii % 2) * 512 + 512]
                    if ii >= 2:
                        pg2, ph2, pt2 = fin_order[ii - 2]
                        saq_wait(vector, f"ost_{pt2}_{ph2}")
                    vector.drain()
                    vector.scalar_tensor_tensor(osts, ps8[j], 1.0, accs,
                                                ALU.mult, ALU.add)
                    vector.drain()
                    ins = vector.tensor_tensor(osts, osts, xrs, ALU.add)
                    v = s_v.inc(ins)
                    mark(f"ost_{tc}_{half}", s_v, v)
                    mark(f"fin_{tc}_{half}", s_v, v)
                    mark(f"ibk_{tc}_{half}", s_v, v)
                    bank_done(j, f"ibk_{tc}_{half}")


    _emit_dma_queues()
    es.close()
    return nc


w_h12_slot = {}


_orig_build_program = build_program


def build_program(TOK=2048, _orig=_orig_build_program):
    w_h12_slot.clear()
    return _orig(TOK)


def kernel(L0_attn_hidden_states, L0_attn_in_proj_w, L0_attn_conv_w,
           L0_attn_conv_b, L0_attn_ln_g, L0_attn_ln_b, L0_attn_fc_w,
           L0_attn_fc_b, L0_attn_cproj_w, L0_attn_cproj_b, L0_attn_out_w,
           L0_ffn_w1_w, L0_ffn_w1_s, L0_ffn_w2_w, L0_ffn_w2_s,
           L0_ffn_h1_w, L0_ffn_h1_s, L0_ffn_h2_w, L0_ffn_h2_s,
           L0_ffn_w3_w, L0_ffn_w3_s):
    TOK = (B * SEQ) // NCORE
    x = np.ascontiguousarray(np.asarray(L0_attn_hidden_states, dtype=np.float32))
    f32 = lambda a: np.ascontiguousarray(np.asarray(a, dtype=np.float32))
    shared = {
        "inprojT": lhsT_panels(f32(L0_attn_in_proj_w)),
        "convP": conv_panels(f32(L0_attn_conv_w)),
        "fcT": lhsT_panels(f32(L0_attn_fc_w)),
        "cprojT": lhsT_panels(f32(L0_attn_cproj_w)),
        "outwR": rhs_chunks(f32(L0_attn_out_w)),
        "w12T": np.ascontiguousarray(np.stack(
            [lhsT_panels(f32(L0_ffn_w1_w)), lhsT_panels(f32(L0_ffn_w2_w))],
            axis=2)),
        "h1T": lhsT_panels(f32(L0_ffn_h1_w)),
        "h2T": lhsT_panels(f32(L0_ffn_h2_w)),
        "w3R": rhs_chunks(f32(L0_ffn_w3_w)),
        "w3sb": np.ascontiguousarray(
            np.broadcast_to(f32(L0_ffn_w3_s).reshape(1, H), (128, H))),
        "pm": np.concatenate([np.ones((1, 128), np.float32),
                              -np.ones((1, 128), np.float32)], axis=1),
        "onesf": np.ones((128, 1), np.float32),
        "identf": np.eye(128, dtype=np.float32),
        "consts": np.ascontiguousarray(np.broadcast_to(
            np.array([EPS, 0.0, 1.0, -1.0, 0.0, 0.0, 0.0, 0.0], np.float32),
            (128, 8))),
    }
    params = np.zeros((128, 200), np.float32)
    params[:, 0:16] = vec128(f32(L0_attn_conv_b))
    params[:, 16:32] = vec128(f32(L0_attn_ln_g))
    params[:, 32:48] = vec128(f32(L0_attn_ln_b))
    params[:, 48:56] = vec128(f32(L0_attn_fc_b))
    params[:, 56:72] = vec128(f32(L0_attn_cproj_b))
    params[:, 72:104] = vec128(f32(L0_ffn_w1_s))
    params[:, 104:136] = vec128(f32(L0_ffn_w2_s))
    params[:, 136:168] = vec128(f32(L0_ffn_h1_s))
    params[:, 168:200] = vec128(f32(L0_ffn_h2_s))
    shared["params"] = params

    cores_per_b = NCORE // B
    in_maps = []
    for c in range(NCORE):
        b, chunk = divmod(c, cores_per_b)
        s = chunk * TOK
        if chunk == 0:
            halo = np.zeros((128, H), np.float32)
        else:
            halo = x[b, s - 128:s]
        xc = np.ascontiguousarray(np.concatenate([halo, x[b, s:s + TOK]], axis=0))
        m = dict(shared)
        m["x"] = xc
        in_maps.append(m)

    nc = build_program(TOK)
    res = run_bass_kernel_spmd(nc, in_maps, core_ids=list(range(NCORE)))
    out = np.empty((B, SEQ, H), np.float32)
    for c in range(NCORE):
        b, chunk = divmod(c, cores_per_b)
        out[b, chunk * TOK:(chunk + 1) * TOK] = res.results[c]["out"]
    return out
